# version 10
# baseline (speedup 1.0000x reference)
from contextlib import ExitStack

import numpy as np
import ml_dtypes

import concourse.bass as bass
import concourse.mybir as mybir
from concourse.bass_utils import run_bass_kernel_spmd

F32 = mybir.dt.float32
BF16 = mybir.dt.bfloat16
AF = mybir.ActivationFunctionType
ALU = mybir.AluOpType
AX = mybir.AxisListType

D = 128
HID = 2048
FFN = 8192
S = 8192
NB = 128
NTM = 16
NFC = 16
NSW = 4
SM_SCALE = float(D ** -0.5)
ESHIFT = -12.0
BIG = 1.0e30
NWC = 2 * (4 + 4 + 4 + 16)

bf16 = ml_dtypes.bfloat16


def _host_consts():
    qt = np.arange(64)[:, None, None]
    q = np.arange(128)[None, :, None]
    n = np.arange(NB)[None, None, :]
    qb = qt * 2 + q // 64
    gmask = np.where(n == qb, BIG, np.where(n > qb, -BIG, 0.0)).astype(np.float32)
    kk = np.arange(128)[:, None, None]
    o = np.arange(4)[None, :, None]
    u = np.arange(512)[None, None, :]
    cmask = np.where(128 * o + kk <= u, 0.0, -BIG).astype(np.float32).reshape(128, 4 * 512)
    j = np.arange(64)[:, None, None]
    n2 = np.arange(NB)[None, :, None]
    kk2 = np.arange(128)[None, None, :]
    indsel = np.where(n2 == 2 * j + kk2 // 64, BIG, 0.0).astype(bf16)
    ident = np.eye(128, dtype=np.float32)
    return gmask, cmask, indsel, ident


def build_fused():
    nc = bass.Bass("TRN2", target_bir_lowering=False)
    x_fm = nc.dram_tensor("x_fm", [HID, S], F32, kind="ExternalInput").ap()
    wq_s = nc.dram_tensor("wq_s", [HID, 512], F32, kind="ExternalInput").ap()
    wk_s = nc.dram_tensor("wk_s", [HID, 512], F32, kind="ExternalInput").ap()
    wv_s = nc.dram_tensor("wv_s", [HID, 512], F32, kind="ExternalInput").ap()
    gmask_d = nc.dram_tensor("gmask", [64, 128, 128], F32, kind="ExternalInput").ap()
    cmask_d = nc.dram_tensor("cmask", [128, 2048], F32, kind="ExternalInput").ap()
    indsel_d = nc.dram_tensor("indsel", [64, 128, 128], BF16, kind="ExternalInput").ap()
    ident_d = nc.dram_tensor("ident", [128, 128], F32, kind="ExternalInput").ap()
    x_sl = nc.dram_tensor("x_sl", [HID, 2048], F32, kind="ExternalInput").ap()
    wo_d = nc.dram_tensor("wo", [HID, HID], F32, kind="ExternalInput").ap()
    wgT_d = nc.dram_tensor("wgT", [HID, FFN], F32, kind="ExternalInput").ap()
    wuT_d = nc.dram_tensor("wuT", [HID, FFN], F32, kind="ExternalInput").ap()
    wdT_d = nc.dram_tensor("wdT", [FFN, HID], F32, kind="ExternalInput").ap()
    o_loc = nc.dram_tensor("o_loc", [512, S], BF16, kind="Internal").ap()
    x_bf = nc.dram_tensor("x_bf", [128, NFC * S], BF16, kind="Internal").ap()
    o_ag = nc.dram_tensor("o_ag", [HID, S], BF16, kind="Internal").ap()
    wo_bf = nc.dram_tensor("wo_bf", [HID, HID], BF16, kind="Internal").ap()
    wgT_bf = nc.dram_tensor("wgT_bf", [HID, FFN], BF16, kind="Internal").ap()
    wuT_bf = nc.dram_tensor("wuT_bf", [HID, FFN], BF16, kind="Internal").ap()
    wdT_bf = nc.dram_tensor("wdT_bf", [FFN, HID], BF16, kind="Internal").ap()
    out_d = nc.dram_tensor("out_fm", [HID, 2048], F32, kind="ExternalOutput").ap()

    es = ExitStack()
    Sb = es.enter_context
    with es:
        ARENA_COLS = 104448
        arena_t = Sb(nc.sbuf_tensor("arena", [128, ARENA_COLS], BF16))
        arena = arena_t[:]
        _off = [0]

        def take(shape, dtype):
            elems = 1
            for sdim in shape[1:]:
                elems *= sdim
            units = elems * (2 if dtype == F32 else 1)
            off = (_off[0] + 15) & ~15
            _off[0] = off + units
            assert _off[0] <= ARENA_COLS, f"arena overflow at {_off[0]}"
            ap = arena[:, off:off + units]
            if dtype == F32:
                ap = ap.bitcast(F32)
            if len(shape) == 3:
                ap = ap.rearrange("p (a b) -> p a b", a=shape[1])
            if shape[0] == 1:
                ap = ap[0:1]
            return ap

        ones_bf = take([128, 16], BF16)[:, 0:1]
        ones_f = take([1, 128], F32)
        nones_f = take([1, 128], F32)
        consts = take([128, 2], F32)
        epsc = take([1, 16], F32)[0:1, 0:1]
        ident_sb = take([128, 128], F32)
        PH_BASE = _off[0]

        xbf = take([128, 4, NFC * 512], BF16)
        q_buf = take([128, S], BF16)
        k_buf = take([128, S], BF16)
        v_buf = take([128, 64 * 132], BF16)
        wq_sb = take([128, NFC * 128], BF16)
        wk_sb = take([128, NFC * 128], BF16)
        wv_sb = take([128, NFC * 128], BF16)
        kg_sum = take([128, NB], F32)
        kg_bf = take([128, NB], BF16)
        gm_sb = take([128, 512], F32)
        top8 = take([128, 32], F32)
        gb_sb = take([128, 512], F32)
        gbT_buf = take([128, S], BF16)
        gmask_sb = take([128, 2 * 512], F32)
        cmask_sb = take([128, 2048], F32)
        ind_sb = take([128, 64 * 128], BF16)
        pT = take([128, 4, 1024], BF16)
        o_tok = take([128, 2, 512], BF16)
        o_sb1 = take([128, 2, 512], BF16)
        rcpcol = take([128, 8], F32)
        ident_bf = take([128, 128], BF16)
        ph1_end = _off[0]

        _off[0] = PH_BASE
        o_sb = take([128, NFC * 512], BF16)
        y_sb = take([128, NFC * 512], F32)
        y1_sb = take([128, NFC * 512], F32)
        y2_sb = y_sb
        xbf_sb = take([128, NFC * 512], BF16)
        h_sb = take([128, 64 * 512], BF16)
        ybf_db = take([128, 2, 512], BF16)
        ysq_db = take([128, 2, 512], BF16)
        xch_db = take([128, 2, 512], F32)
        wo_db = take([128, 2, NFC * 128], BF16)
        wg_db = take([128, 2, 8 * 128], BF16)
        wu_db = take([128, 2, 8 * 128], BF16)
        wd_db = take([128, 2, 16 * 128], BF16)
        out_db = take([128, 2, 512], F32)
        mrow = take([1, 512], F32)
        rows2 = take([1, 1024], F32)
        ph2_end = _off[0]

        P01 = Sb(nc.psum_tensor("P01", [128, 1024], F32))
        P23 = Sb(nc.psum_tensor("P23", [128, 1024], F32))
        P4 = Sb(nc.psum_tensor("P4", [128, 512], F32))
        P5 = Sb(nc.psum_tensor("P5", [128, 512], F32))
        P6 = Sb(nc.psum_tensor("P6", [128, 512], F32))
        P7 = Sb(nc.psum_tensor("P7", [128, 512], F32))

        s_init = Sb(nc.semaphore("s_init"))
        s_cst = Sb(nc.semaphore("s_cst"))
        s_w = Sb(nc.semaphore("s_w"))
        s_x = [Sb(nc.semaphore(f"s_x{j}")) for j in range(4)]
        s_xw = [Sb(nc.semaphore(f"s_xw{j}")) for j in range(4)]
        s_x2 = [Sb(nc.semaphore(f"s_x2{j}")) for j in range(4)]
        s_proj = Sb(nc.semaphore("s_proj"))
        s_evq = Sb(nc.semaphore("s_evq"))
        s_evv = Sb(nc.semaphore("s_evv"))
        s_kg = Sb(nc.semaphore("s_kg"))
        s_gmld = [Sb(nc.semaphore("s_gmld0")), Sb(nc.semaphore("s_gmld1"))]
        s_gmm = Sb(nc.semaphore("s_gmm"))
        s_gdve = Sb(nc.semaphore("s_gdve"))
        s_gtr = Sb(nc.semaphore("s_gtr"))
        s_gbT = Sb(nc.semaphore("s_gbT"))
        s_qk = Sb(nc.semaphore("s_qk"))
        s_msk = Sb(nc.semaphore("s_msk"))
        s_exp = Sb(nc.semaphore("s_exp"))
        s_pv = Sb(nc.semaphore("s_pv"))
        s_vm = Sb(nc.semaphore("s_vm"))
        s_onm = Sb(nc.semaphore("s_onm"))
        s_otr = Sb(nc.semaphore("s_otr"))
        s_om = Sb(nc.semaphore("s_om"))
        s_st = [Sb(nc.semaphore("s_st0")), Sb(nc.semaphore("s_st1"))]
        s_cc = Sb(nc.semaphore("s_cc"))
        s_wc = Sb(nc.semaphore("s_wc"))
        s_ld = Sb(nc.semaphore("s_ld"))
        s_xc = [Sb(nc.semaphore("s_xc0")), Sb(nc.semaphore("s_xc1"))]
        s_wo = [Sb(nc.semaphore("s_wo0")), Sb(nc.semaphore("s_wo1"))]
        s_wg = [Sb(nc.semaphore("s_wg0")), Sb(nc.semaphore("s_wg1"))]
        s_wd = [Sb(nc.semaphore("s_wd0")), Sb(nc.semaphore("s_wd1"))]
        s_at = Sb(nc.semaphore("s_at"))
        s_y = Sb(nc.semaphore("s_y"))
        s_yb = Sb(nc.semaphore("s_yb"))
        s_sm = Sb(nc.semaphore("s_sm"))
        s_ln = Sb(nc.semaphore("s_ln"))
        s_sq = Sb(nc.semaphore("s_sq"))
        s_ab = Sb(nc.semaphore("s_ab"))
        s_n1 = Sb(nc.semaphore("s_n1"))
        s_n2 = Sb(nc.semaphore("s_n2"))
        s_xb = Sb(nc.semaphore("s_xb"))
        s_si = Sb(nc.semaphore("s_si"))
        s_h = Sb(nc.semaphore("s_h"))
        s_dnh = Sb(nc.semaphore("s_dnh"))
        s_guh = Sb(nc.semaphore("s_guh"))
        s_y2 = Sb(nc.semaphore("s_y2"))
        t_st = [Sb(nc.semaphore("t_st0")), Sb(nc.semaphore("t_st1"))]

        blk = Sb(nc.Block())

        def grp_count(qm):
            return 2 * (qm + 1)

        def gidx(hp, qm, jg):
            return hp * 272 + qm * (qm + 1) + jg

        def qm_idx(hp, qm):
            return hp * 16 + qm

        wcast_chunks = []
        for (dst, src, rows_) in ((wo_bf, wo_d, HID), (wgT_bf, wgT_d, HID),
                                  (wuT_bf, wuT_d, HID), (wdT_bf, wdT_d, FFN)):
            for r0 in range(0, rows_, 256):
                wcast_chunks.append((dst, src, r0))
        assert len(wcast_chunks) == NWC

        def sync_body(sync):
            sync.dma_start(cmask_sb, cmask_d).then_inc(s_cst, 16)
            sync.dma_start(ind_sb.rearrange("p (j k) -> p j k", j=64),
                           indsel_d.rearrange("j p k -> p j k")).then_inc(s_cst, 16)
            sync.dma_start(ident_sb, ident_d).then_inc(s_cst, 16)
            for hp in range(4):
                if hp == 0:
                    for tm in range(NTM):
                        sync.wait_ge(s_x[tm % 4], 16 * (tm // 4 + 1))
                        sync.dma_start(x_bf[:, tm * 512 * NFC:(tm + 1) * 512 * NFC],
                                       xbf[:, tm % 4, :]).then_inc(s_xw[tm % 4], 16)
                for b4 in range(16):
                    i = hp * 16 + b4
                    if i >= 2:
                        sync.wait_ge(s_gdve, 9 * (i - 1))
                    sync.dma_start(
                        gmask_sb[:, (b4 % 2) * 512:(b4 % 2) * 512 + 512].rearrange(
                            "p (q n) -> p q n", q=4),
                        gmask_d.rearrange("q p n -> p q n")[:, b4 * 4:b4 * 4 + 4, :],
                    ).then_inc(s_gmld[i % 2], 16)
                for qm in range(16):
                    qi = qm_idx(hp, qm)
                    sync.wait_ge(s_om, qi + 1)
                    sync.dma_start(o_loc[hp * 128:hp * 128 + 128, qm * 512:qm * 512 + 512],
                                   o_sb1[:, qm % 2, :]).then_inc(s_st[qi % 2], 16)
                    if hp < 3 and qm < 4:
                        i2 = (hp + 1) * 16 + qm
                        if qm == 0 and hp == 0:
                            for j in range(4):
                                sync.wait_ge(s_xw[j], 16 * 4)
                        sync.wait_ge(s_proj, i2 - 3)
                        sync.dma_start(xbf[:, i2 % 4, :],
                                       x_bf[:, qm * 512 * NFC:(qm + 1) * 512 * NFC]
                                       ).then_inc(s_x2[i2 % 4], 16)
                if hp < 3:
                    for qm in range(4, 16):
                        i2 = (hp + 1) * 16 + qm
                        sync.wait_ge(s_proj, i2 - 3)
                        sync.dma_start(xbf[:, i2 % 4, :],
                                       x_bf[:, qm * 512 * NFC:(qm + 1) * 512 * NFC]
                                       ).then_inc(s_x2[i2 % 4], 16)
            sync.wait_ge(s_cc, 4)
            sync.wait_ge(s_wc, 16 * NWC)
            for sw in range(NSW):
                def emit_xw(sw2, fc):
                    fi2 = sw2 * NFC + fc
                    if fi2 >= 2:
                        sync.wait_ge(s_y, fi2 - 1)
                        sync.wait_ge(s_at, fi2 - 1)
                    sync.dma_start(
                        xch_db[:, fi2 % 2, :],
                        x_sl[fc * 128:fc * 128 + 128, sw2 * 512:sw2 * 512 + 512],
                    ).then_inc(s_xc[fi2 % 2], 16)
                    sync.dma_start(
                        wo_db[:, fi2 % 2, :].rearrange("p (c d) -> p c d", c=NFC),
                        wo_bf.rearrange("(c p) f -> p c f", p=128)[:, :, fc * 128:fc * 128 + 128],
                    ).then_inc(s_wo[fi2 % 2], 16)

                for fc in range(NFC):
                    if sw >= 1 and fc < 2:
                        continue
                    emit_xw(sw, fc)
                for hc in range(64):
                    for half in range(2):
                        hi = sw * 64 + hc
                        wgi = hi * 2 + half
                        if wgi >= 2:
                            sync.wait_ge(s_guh, wgi - 1)
                        sync.dma_start(
                            wg_db[:, wgi % 2, :].rearrange("p (c d) -> p c d", c=8),
                            wgT_bf.rearrange("(c p) f -> p c f", p=128)[
                                :, half * 8:half * 8 + 8, hc * 128:hc * 128 + 128],
                        ).then_inc(s_wg[wgi % 2], 16)
                        sync.dma_start(
                            wu_db[:, wgi % 2, :].rearrange("p (c d) -> p c d", c=8),
                            wuT_bf.rearrange("(c p) f -> p c f", p=128)[
                                :, half * 8:half * 8 + 8, hc * 128:hc * 128 + 128],
                        ).then_inc(s_wg[wgi % 2], 16)
                if sw < NSW - 1:
                    emit_xw(sw + 1, 0)
                    emit_xw(sw + 1, 1)
                for fc in range(NFC):
                    for q4 in range(4):
                        fi = sw * NFC + fc
                        wdi = fi * 4 + q4
                        if wdi >= 2:
                            sync.wait_ge(s_dnh, wdi - 1)
                        sync.dma_start(
                            wd_db[:, wdi % 2, :].rearrange("p (c d) -> p c d", c=16),
                            wdT_bf.rearrange("(c p) f -> p c f", p=128)[
                                :, q4 * 16:q4 * 16 + 16, fc * 128:fc * 128 + 128],
                        ).then_inc(s_wd[wdi % 2], 16)
        blk.sync(sync_body)

        def gpsimd_body(gpsimd):
            wci = 0

            def head_loads(hp):
                nonlocal wci
                if hp >= 1:
                    gpsimd.wait_ge(s_proj, hp * 16)
                for wdst, wsrc in ((wq_sb, wq_s), (wk_sb, wk_s), (wv_sb, wv_s)):
                    gpsimd.dma_start(
                        wdst.rearrange("p (c d) -> p c d", c=NFC),
                        wsrc.rearrange("(c p) hd -> p c hd", p=128)[:, :, hp * 128:hp * 128 + 128],
                    ).then_inc(s_w, 16)
                if hp >= 1:
                    return
                for tm in range(NTM):
                    i = hp * 16 + tm
                    if i >= 4:
                        gpsimd.wait_ge(s_proj, i - 3)
                        gpsimd.wait_ge(s_xw[i % 4], 16 * (i // 4))
                    gpsimd.dma_start(
                        xbf[:, i % 4, :].rearrange("p (c n) -> p c n", c=NFC),
                        x_fm.rearrange("(c p) s -> p c s", p=128)[:, :, tm * 512:tm * 512 + 512],
                    ).then_inc(s_x[i % 4], 16)


            def head_ag(hp, half=None):
                if half is None:
                    cols = slice(0, S)
                    st_need = 16 * 8 * (hp + 1)
                else:
                    cols = slice(half * 4096, half * 4096 + 4096)
                    st_need = 16 * (8 * hp + 4 * (half + 1))
                gpsimd.wait_ge(s_st[0], st_need)
                gpsimd.wait_ge(s_st[1], st_need)
                gpsimd.collective_compute(
                    "AllGather", ALU.bypass,
                    replica_groups=[[0, 1, 2, 3], [4, 5, 6, 7]],
                    ins=[o_loc[hp * 128:hp * 128 + 128, cols].opt()],
                    outs=[o_ag[hp * 512:hp * 512 + 512, cols].opt()],
                ).then_inc(s_cc, 1)

            def wcast_until(om_hi):
                nonlocal wci
                while wci < NWC and wci * 56 // NWC < om_hi:
                    gpsimd.wait_ge(s_om, wci * 56 // NWC)
                    dst, src, r0 = wcast_chunks[wci]
                    gpsimd.dma_start(dst[r0:r0 + 256, :],
                                     src[r0:r0 + 256, :]).then_inc(s_wc, 16)
                    wci += 1

            head_loads(0)
            head_loads(1)
            wcast_until(14)
            head_ag(0)
            head_loads(2)
            wcast_until(28)
            head_ag(1)
            head_loads(3)
            wcast_until(42)
            head_ag(2)
            wcast_until(57)
            head_ag(3)
            gpsimd.wait_ge(s_cc, 4)
            qtr = gpsimd.partition_id() % 4
            o_ag_r = o_ag.rearrange("(c p) n -> p c n", p=128)
            for sw in range(NSW):
                if sw >= 1:
                    gpsimd.wait_ge(s_at, sw * NFC)
                base = o_ag_r[:, :, bass.ts(qtr, 2048)]
                gpsimd.dma_start(
                    o_sb.rearrange("p (c n) -> p c n", c=NFC),
                    base[:, :, sw * 512:sw * 512 + 512],
                ).then_inc(s_ld, 16)
        blk.gpsimd(gpsimd_body)

        def vector_body(vector):
            vector.memset(ones_bf, 1.0)
            vector.memset(ones_f, 1.0)
            vector.memset(nones_f, -1.0)
            vector.memset(epsc, 1e-5)
            vector.memset(consts[:, 0:1], ESHIFT)
            vector.memset(consts[:, 1:2], SM_SCALE)
            vector.wait_ge(s_cst, 48)
            vector.tensor_copy(ident_bf, ident_sb).then_inc(s_init, 1)
            for hp in range(4):
                if hp >= 1:
                    vector.wait_ge(s_om, hp * 16)
                vector.memset(v_buf.rearrange("p (t c) -> p t c", t=64)[:, :, 128:129],
                              1.0).then_inc(s_vm, 1)
                for tm in range(NTM):
                    i = hp * 16 + tm
                    vector.wait_ge(s_proj, i + 1)
                    qsrc = P01 if tm % 2 == 0 else P23
                    vector.tensor_copy(q_buf[:, tm * 512:tm * 512 + 512], qsrc[:, 0:512])
                    vector.tensor_copy(k_buf[:, tm * 512:tm * 512 + 512],
                                       qsrc[:, 512:1024]).then_inc(s_evq, 1)
                vector.wait_ge(s_evq, (hp + 1) * 16)
                vector.tensor_reduce(kg_sum, k_buf.rearrange("p (a b) -> p a b", b=64),
                                     AX.X, ALU.add).then_inc(s_kg, 1)
                vector.wait_ge(s_kg, 2 * hp + 1)
                vector.tensor_copy(kg_bf, kg_sum).then_inc(s_kg, 1)
                for b4 in range(16):
                    i = hp * 16 + b4
                    vector.wait_ge(s_gmm, i + 1)
                    vector.wait_ge(s_gmld[i % 2], 16 * (i // 2 + 1))
                    gsl = (P4 if b4 % 2 == 0 else P5)[:]
                    vector.tensor_tensor(gm_sb, gsl,
                                         gmask_sb[:, (b4 % 2) * 512:(b4 % 2) * 512 + 512],
                                         ALU.add).then_inc(s_gdve, 1)
                    for c in range(4):
                        if c == 0:
                            vector.wait_ge(s_gdve, 9 * i + 1)
                        vector.max(top8[:, c * 8:c * 8 + 8],
                                   gm_sb[:, c * 128:c * 128 + 128]).then_inc(s_gdve, 1)
                    for c in range(4):
                        vector.wait_ge(s_gdve, 9 * i + 2 + c)
                        vector.tensor_scalar(gb_sb[:, c * 128:c * 128 + 128],
                                             gm_sb[:, c * 128:c * 128 + 128],
                                             top8[:, c * 8 + 7:c * 8 + 8], 1.0,
                                             ALU.is_ge, ALU.subtract).then_inc(s_gdve, 1)
                    vector.wait_ge(s_gtr, 4 * i + 4)
                    tsl = (P6 if b4 % 2 == 0 else P7)[:]
                    vector.tensor_copy(gbT_buf[:, b4 * 512:b4 * 512 + 512],
                                       tsl).then_inc(s_gbT, 1)
                for qm in range(16):
                    qi = qm_idx(hp, qm)
                    ng = grp_count(qm)
                    for jg in (ng - 2, ng - 1):
                        g = gidx(hp, qm, jg)
                        vector.wait_ge(s_qk, g + 1)
                        sc = P01 if jg % 2 == 0 else P23
                        for t in range(2):
                            off = (2 * jg + t) - 4 * qm
                            ins = vector.tensor_tensor(
                                sc[:, t * 512:t * 512 + 512], sc[:, t * 512:t * 512 + 512],
                                cmask_sb[:, off * 512:off * 512 + 512], ALU.add)
                            if t == 1:
                                ins.then_inc(s_msk, 1)
                    vector.wait_ge(s_pv, gidx(hp, qm, ng - 1) + 1)
                    OACC = (P4, P5, P6, P7)
                    for c in range(4):
                        vector.reciprocal(rcpcol[:, c:c + 1],
                                          OACC[c][:, 128:129]).then_inc(s_onm, 1)
                    for c in range(4):
                        vector.wait_ge(s_onm, 8 * qi + c + 1)
                        vector.tensor_scalar(o_tok[:, qm % 2, c * 128:c * 128 + 128],
                                             OACC[c][:, 0:128], rcpcol[:, c:c + 1], None,
                                             ALU.mult).then_inc(s_onm, 1)
                    for c in range(4):
                        vector.wait_ge(s_otr, 4 * qi + c + 1)
                        if c == 0 and qi >= 2:
                            vector.wait_ge(s_st[qi % 2], 16 * (qi // 2))
                        obt = OACC[c][:, 256:512].bitcast(BF16)[:, 0:128]
                        ins = vector.tensor_copy(o_sb1[:, qm % 2, c * 128:c * 128 + 128], obt)
                        if c == 3:
                            ins.then_inc(s_om, 1)

            def ln_rows(li):
                vector.wait_ge(s_sm, (li + 1) * NFC)
                if li >= 1:
                    vector.wait_ge(s_ab, 2 * li)
                vector.tensor_scalar(mrow, P6[0:1, 0:512], 1.0 / 2048.0, None,
                                     ALU.mult).then_inc(s_ln, 1)
                vector.wait_ge(s_ln, li * 6 + 1)
                vector.tensor_scalar(rows2[0:1, 0:512], P7[0:1, 0:512], 1.0 / 2048.0, None,
                                     ALU.mult).then_inc(s_ln, 1)
                vector.wait_ge(s_ln, li * 6 + 2)
                vector.tensor_tensor(rows2[0:1, 512:1024], mrow, mrow,
                                     ALU.mult).then_inc(s_ln, 1)
                vector.wait_ge(s_ln, li * 6 + 3)
                vector.tensor_tensor(rows2[0:1, 0:512], rows2[0:1, 0:512],
                                     rows2[0:1, 512:1024], ALU.subtract).then_inc(s_ln, 1)
                vector.wait_ge(s_sq, li + 1)
                vector.reciprocal(rows2[0:1, 0:512], rows2[0:1, 512:1024]).then_inc(s_ln, 1)
                vector.wait_ge(s_ln, li * 6 + 5)
                vector.tensor_tensor(rows2[0:1, 512:1024], mrow, rows2[0:1, 0:512],
                                     ALU.mult).then_inc(s_ln, 1)

            for sw in range(NSW):
                for fc in range(NFC):
                    fi = sw * NFC + fc
                    vector.wait_ge(s_at, fi + 1)
                    vector.wait_ge(s_xc[fi % 2], 16 * (fi // 2 + 1))
                    asrc = P4[:] if fc % 2 == 0 else P5[:]
                    vector.tensor_tensor(y_sb[:, fc * 512:fc * 512 + 512], asrc,
                                         xch_db[:, fi % 2, :], ALU.add).then_inc(s_y, 1)
                ln_rows(2 * sw)
                for fc in range(NFC):
                    li = 2 * sw
                    gi = li * NFC + fc
                    vector.wait_ge(s_ab, 2 * li + 2)
                    vector.tensor_tensor(y1_sb[:, fc * 512:fc * 512 + 512],
                                         y_sb[:, fc * 512:fc * 512 + 512],
                                         P01[:, 0:512], ALU.mult).then_inc(s_n1, 1)
                    vector.wait_ge(s_n1, gi + 1)
                    vector.tensor_tensor(y1_sb[:, fc * 512:fc * 512 + 512],
                                         y1_sb[:, fc * 512:fc * 512 + 512],
                                         P01[:, 512:1024], ALU.add).then_inc(s_n2, 1)
                for hc in range(64):
                    hi = sw * 64 + hc
                    vector.wait_ge(s_si, hi + 1)
                    usrc = P23[:, 512:1024] if hc % 2 == 0 else P5[:]
                    vector.tensor_tensor(h_sb[:, hc * 512:hc * 512 + 512],
                                         h_sb[:, hc * 512:hc * 512 + 512], usrc,
                                         ALU.mult).then_inc(s_h, 1)
                for fc in range(NFC):
                    fi = sw * NFC + fc
                    vector.wait_ge(s_dnh, 4 * (fi + 1))
                    fsrc = P23[:, 0:512] if fc % 2 == 0 else P23[:, 512:1024]
                    vector.tensor_tensor(y2_sb[:, fc * 512:fc * 512 + 512], fsrc,
                                         y1_sb[:, fc * 512:fc * 512 + 512],
                                         ALU.add).then_inc(s_y2, 1)
                ln_rows(2 * sw + 1)
                for fc in range(NFC):
                    li = 2 * sw + 1
                    gi = li * NFC + fc
                    fi = sw * NFC + fc
                    vector.wait_ge(s_ab, 2 * li + 2)
                    if fi >= 2:
                        vector.wait_ge(t_st[fi % 2], 16 * (fi // 2))
                    vector.tensor_tensor(out_db[:, fi % 2, :],
                                         y2_sb[:, fc * 512:fc * 512 + 512],
                                         P01[:, 0:512], ALU.mult).then_inc(s_n1, 1)
                    vector.wait_ge(s_n1, gi + 1)
                    vector.tensor_tensor(out_db[:, fi % 2, :], out_db[:, fi % 2, :],
                                         P01[:, 512:1024], ALU.add).then_inc(s_n2, 1)
        blk.vector(vector_body)

        def scalar_body(scalar):
            scalar.wait_ge(s_init, 1)
            for hp in range(4):
                for tm in range(NTM):
                    i = hp * 16 + tm
                    scalar.wait_ge(s_proj, i + 1)
                    vsrc = P4 if tm % 2 == 0 else P5
                    vdst3 = v_buf[:, tm * 528:tm * 528 + 528].rearrange(
                        "p (t c) -> p t c", t=4)[:, :, 0:128]
                    scalar.copy(vdst3, vsrc[:].rearrange("p (t c) -> p t c", t=4)
                                ).then_inc(s_evv, 1)
                for qm in range(16):
                    qi = qm_idx(hp, qm)
                    ng = grp_count(qm)
                    for jg in range(ng):
                        g = gidx(hp, qm, jg)
                        if jg >= ng - 2:
                            scalar.wait_ge(s_msk, hp * 32 + qm * 2 + (jg - (ng - 2)) + 1)
                        else:
                            scalar.wait_ge(s_qk, g + 1)
                        sc = P01 if jg % 2 == 0 else P23
                        if g >= 4:
                            scalar.wait_ge(s_pv, g - 3)
                        scalar.activation(pT[:, g % 4, :], sc[:], AF.Exp,
                                          bias=consts[:, 0:1],
                                          scale=consts[:, 1:2]).then_inc(s_exp, 1)

            def sqrt_row(li):
                scalar.wait_ge(s_ln, li * 6 + 4)
                scalar.activation(rows2[0:1, 512:1024], rows2[0:1, 0:512], AF.Sqrt,
                                  bias=epsc).then_inc(s_sq, 1)

            for sw in range(NSW):
                for fc in range(NFC):
                    li = 2 * sw
                    gi = li * NFC + fc
                    scalar.wait_ge(s_y, sw * NFC + fc + 1)
                    if gi >= 2:
                        scalar.wait_ge(s_sm, gi - 1)
                    scalar.activation(ybf_db[:, gi % 2, :], y_sb[:, fc * 512:fc * 512 + 512],
                                      AF.Identity).then_inc(s_yb, 1)
                    scalar.activation(ysq_db[:, gi % 2, :], y_sb[:, fc * 512:fc * 512 + 512],
                                      AF.Square).then_inc(s_yb, 1)
                sqrt_row(2 * sw)
                for fc in range(NFC):
                    gi = 2 * sw * NFC + fc
                    scalar.wait_ge(s_n2, gi + 1)
                    scalar.activation(xbf_sb[:, fc * 512:fc * 512 + 512],
                                      y1_sb[:, fc * 512:fc * 512 + 512],
                                      AF.Identity).then_inc(s_xb, 1)
                for hc in range(64):
                    hi = sw * 64 + hc
                    scalar.wait_ge(s_guh, 2 * (hi + 1))
                    gsrc = P23[:, 0:512] if hc % 2 == 0 else P4[:]
                    scalar.activation(h_sb[:, hc * 512:hc * 512 + 512], gsrc,
                                      AF.Silu).then_inc(s_si, 1)
                for fc in range(NFC):
                    li = 2 * sw + 1
                    gi = li * NFC + fc
                    scalar.wait_ge(s_y2, sw * NFC + fc + 1)
                    if gi >= 2:
                        scalar.wait_ge(s_sm, gi - 1)
                    scalar.activation(ybf_db[:, gi % 2, :], y2_sb[:, fc * 512:fc * 512 + 512],
                                      AF.Identity).then_inc(s_yb, 1)
                    scalar.activation(ysq_db[:, gi % 2, :], y2_sb[:, fc * 512:fc * 512 + 512],
                                      AF.Square).then_inc(s_yb, 1)
                sqrt_row(2 * sw + 1)
                for fc in range(NFC):
                    li = 2 * sw + 1
                    gi = li * NFC + fc
                    fi = sw * NFC + fc
                    scalar.wait_ge(s_n2, gi + 1)
                    scalar.dma_start(out_d[fc * 128:fc * 128 + 128, sw * 512:sw * 512 + 512],
                                     out_db[:, fi % 2, :]).then_inc(t_st[fi % 2], 16)
            scalar.wait_ge(t_st[0], 16 * NSW * NFC // 2)
            scalar.wait_ge(t_st[1], 16 * NSW * NFC // 2)
        blk.scalar(scalar_body)

        def tensor_body(tensor):
            tensor.wait_ge(s_init, 1)
            tensor.wait_ge(s_cst, 48)
            for hp in range(4):
                tensor.wait_ge(s_w, (hp * 3 + 3) * 16)
                for tm in range(NTM):
                    i = hp * 16 + tm
                    if hp == 0:
                        tensor.wait_ge(s_x[i % 4], 16 * (i // 4 + 1))
                    else:
                        tensor.wait_ge(s_x2[i % 4], 16 * ((i - 16) // 4 + 1))
                    if tm == 0 and hp >= 1:
                        tensor.wait_ge(s_om, hp * 16)
                    if i >= 2:
                        tensor.wait_ge(s_evq, i - 1)
                        tensor.wait_ge(s_evv, i - 1)
                    xt = xbf[:, i % 4, :]
                    qdst = P01 if tm % 2 == 0 else P23
                    vdst = P4 if tm % 2 == 0 else P5
                    for fc in range(NFC):
                        tensor.matmul(qdst[:, 0:512], wq_sb[:, fc * 128:fc * 128 + 128],
                                      xt[:, fc * 512:fc * 512 + 512],
                                      start=(fc == 0), stop=(fc == NFC - 1))
                        tensor.matmul(vdst[:, 0:128],
                                      xt[:, fc * 512:fc * 512 + 128],
                                      wv_sb[:, fc * 128:fc * 128 + 128],
                                      start=(fc == 0), stop=False, skip_group_check=True)
                        tensor.matmul(qdst[:, 512:1024], wk_sb[:, fc * 128:fc * 128 + 128],
                                      xt[:, fc * 512:fc * 512 + 512],
                                      start=(fc == 0), stop=(fc == NFC - 1))
                        tensor.matmul(vdst[:, 128:256],
                                      xt[:, fc * 512 + 128:fc * 512 + 256],
                                      wv_sb[:, fc * 128:fc * 128 + 128],
                                      start=False, stop=False, skip_group_check=True)
                        tensor.matmul(vdst[:, 256:384],
                                      xt[:, fc * 512 + 256:fc * 512 + 384],
                                      wv_sb[:, fc * 128:fc * 128 + 128],
                                      start=False, stop=False, skip_group_check=True)
                        ins = tensor.matmul(
                            vdst[:, 384:512],
                            xt[:, fc * 512 + 384:fc * 512 + 512],
                            wv_sb[:, fc * 128:fc * 128 + 128],
                            start=False, stop=(fc == NFC - 1),
                            skip_group_check=True)
                        if fc == NFC - 1:
                            ins.then_inc(s_proj, 1)
                tensor.wait_ge(s_kg, 2 * (hp + 1))
                tensor.wait_ge(s_evq, (hp + 1) * 16)
                tensor.wait_ge(s_evv, (hp + 1) * 16)
                for b4 in range(16):
                    i = hp * 16 + b4
                    gdst = (P4 if b4 % 2 == 0 else P5)[:]
                    if i >= 2:
                        tensor.wait_ge(s_gdve, 9 * (i - 1))
                    for c in range(4):
                        ins = tensor.matmul(gdst[:, c * 128:c * 128 + 128],
                                            q_buf[:, (b4 * 4 + c) * 128:(b4 * 4 + c) * 128 + 128],
                                            kg_bf, start=(c == 0), stop=(c == 3),
                                            skip_group_check=True)
                        if c == 3:
                            ins.then_inc(s_gmm, 1)
                    tdst = (P6 if b4 % 2 == 0 else P7)[:]
                    if i >= 2:
                        tensor.wait_ge(s_gbT, i - 1)
                    for c in range(4):
                        tensor.wait_ge(s_gdve, 9 * i + 6 + c)
                        tensor.transpose(tdst[:, c * 128:c * 128 + 128],
                                         gb_sb[:, c * 128:c * 128 + 128],
                                         ident_sb).then_inc(s_gtr, 1)
                tensor.wait_ge(s_gbT, (hp + 1) * 16)
                tensor.wait_ge(s_vm, hp + 1)
                OACC = (P4, P5, P6, P7)

                def emit_pv(qm, jg_pv):
                    qi = qm_idx(hp, qm)
                    ng = grp_count(qm)
                    gp = gidx(hp, qm, jg_pv)
                    tensor.wait_ge(s_exp, gp + 1)
                    if jg_pv == 0 and qi > 0:
                        tensor.wait_ge(s_onm, 8 * qi)
                        tensor.wait_ge(s_om, qi)
                    pt = pT[:, gp % 4, :]
                    for tt in range(2):
                        jj = 2 * jg_pv + tt
                        last = (jg_pv == ng - 1 and tt == 1)
                        for c in range(4):
                            ins = tensor.matmul(
                                OACC[c][:, 0:129],
                                pt[:, tt * 512 + c * 128:tt * 512 + c * 128 + 128],
                                v_buf[:, jj * 132:jj * 132 + 129],
                                start=(jg_pv == 0 and tt == 0), stop=last,
                                skip_group_check=True)
                            if tt == 1 and c == 3:
                                ins.then_inc(s_pv, 1)

                def emit_otr(qm):
                    qi = qm_idx(hp, qm)
                    for c in range(4):
                        tensor.wait_ge(s_onm, 8 * qi + 4 + c + 1)
                        obt = OACC[c][:, 256:512].bitcast(BF16)[:, 0:128]
                        tensor.transpose(obt, o_tok[:, qm % 2, c * 128:c * 128 + 128],
                                         ident_bf).then_inc(s_otr, 1)

                for qm in range(16):
                    qi = qm_idx(hp, qm)
                    ng = grp_count(qm)
                    qrhs = q_buf[:, qm * 512:qm * 512 + 512]
                    for jg in range(ng):
                        g = gidx(hp, qm, jg)
                        sc = P01 if jg % 2 == 0 else P23
                        if g >= 2:
                            tensor.wait_ge(s_exp, g - 1)
                        for t in range(2):
                            j = 2 * jg + t
                            tensor.matmul(sc[:, t * 512:t * 512 + 512],
                                          k_buf[:, j * 128:j * 128 + 128], qrhs,
                                          start=True, stop=False)
                            ins = tensor.matmul(sc[:, t * 512:t * 512 + 512],
                                                ind_sb[:, j * 128:j * 128 + 128],
                                                gbT_buf[:, qm * 512:qm * 512 + 512],
                                                start=False, stop=True)
                            if t == 1:
                                ins.then_inc(s_qk, 1)
                        if jg == 1 and qm >= 1:
                            emit_otr(qm - 1)
                        if jg >= 2:
                            emit_pv(qm, jg - 2)
                        if jg == ng - 1:
                            emit_pv(qm, jg - 1)
                            emit_pv(qm, jg)
                    if qm == 15:
                        emit_otr(qm)

            def stats_mm(sw, ln, fc):
                li = 2 * sw + ln
                gi = li * NFC + fc
                tensor.wait_ge(s_yb, 2 * (gi + 1))
                if fc == 0 and li >= 1:
                    tensor.wait_ge(s_ln, (li - 1) * 6 + 2)
                tensor.matmul(P6[0:1, 0:512], ones_bf,
                              ybf_db[:, gi % 2, :],
                              start=(fc == 0), stop=(fc == NFC - 1), skip_group_check=True)
                tensor.matmul(P7[0:1, 0:512], ones_bf,
                              ysq_db[:, gi % 2, :],
                              start=(fc == 0), stop=(fc == NFC - 1),
                              skip_group_check=True).then_inc(s_sm, 1)

            def ab_mm(sw, ln):
                li = 2 * sw + ln
                tensor.wait_ge(s_ln, li * 6 + 5)
                if li >= 1:
                    tensor.wait_ge(s_n2, li * NFC)
                tensor.matmul(P01[:, 0:512], ones_f, rows2[0:1, 0:512],
                              start=True, stop=True)
                tensor.wait_ge(s_ln, li * 6 + 6)
                tensor.matmul(P01[:, 512:1024], nones_f, rows2[0:1, 512:1024],
                              start=True, stop=True).then_inc(s_ab, 2)

            for sw in range(NSW):
                tensor.wait_ge(s_ld, 16 * (sw + 1))
                for fc in range(NFC):
                    fi = sw * NFC + fc
                    tensor.wait_ge(s_wo[fi % 2], 16 * (fi // 2 + 1))
                    if fc == 0 and sw == 0:
                        tensor.wait_ge(s_om, 64)
                    if fc == 0 and sw >= 1:
                        tensor.wait_ge(s_h, sw * 64)
                    if fi >= 2:
                        tensor.wait_ge(s_y, fi - 1)
                    adst = P4[:] if fc % 2 == 0 else P5[:]
                    ws = wo_db[:, fi % 2, :]
                    for oc in range(NFC):
                        ins = tensor.matmul(adst, ws[:, oc * 128:oc * 128 + 128],
                                            o_sb[:, oc * 512:oc * 512 + 512],
                                            start=(oc == 0), stop=(oc == NFC - 1),
                                            skip_group_check=True)
                        if oc == NFC - 1:
                            ins.then_inc(s_at, 1)
                    if fc >= 2:
                        stats_mm(sw, 0, fc - 2)
                stats_mm(sw, 0, NFC - 2)
                stats_mm(sw, 0, NFC - 1)
                ab_mm(sw, 0)
                for hc in range(64):
                    hi = sw * 64 + hc
                    if hc == 0 and sw >= 1:
                        tensor.wait_ge(s_y2, sw * NFC)
                    if hi >= 2:
                        tensor.wait_ge(s_h, hi - 1)
                    gdst = P23[:, 0:512] if hc % 2 == 0 else P4[:]
                    udst = P23[:, 512:1024] if hc % 2 == 0 else P5[:]
                    for half in range(2):
                        wgi = hi * 2 + half
                        tensor.wait_ge(s_wg[wgi % 2], 32 * (wgi // 2 + 1))
                        for ff in range(8):
                            fc = half * 8 + ff
                            if hc == 0:
                                tensor.wait_ge(s_xb, sw * NFC + fc + 1)
                            tensor.matmul(gdst, wg_db[:, wgi % 2, ff * 128:ff * 128 + 128],
                                          xbf_sb[:, fc * 512:fc * 512 + 512],
                                          start=(fc == 0), stop=(fc == NFC - 1),
                                          skip_group_check=True)
                            ins = tensor.matmul(udst, wu_db[:, wgi % 2, ff * 128:ff * 128 + 128],
                                                xbf_sb[:, fc * 512:fc * 512 + 512],
                                                start=(fc == 0), stop=(fc == NFC - 1),
                                                skip_group_check=True)
                            if ff == 7:
                                ins.then_inc(s_guh, 1)
                for fc in range(NFC):
                    fi = sw * NFC + fc
                    if fc == 0:
                        tensor.wait_ge(s_h, (sw + 1) * 64)
                    if fi >= 2:
                        tensor.wait_ge(s_y2, fi - 1)
                    fdst = P23[:, 0:512] if fc % 2 == 0 else P23[:, 512:1024]
                    for q4 in range(4):
                        wdi = fi * 4 + q4
                        tensor.wait_ge(s_wd[wdi % 2], 16 * (wdi // 2 + 1))
                        for hh in range(16):
                            hc = q4 * 16 + hh
                            ins = tensor.matmul(
                                fdst, wd_db[:, wdi % 2, hh * 128:hh * 128 + 128],
                                h_sb[:, hc * 512:hc * 512 + 512],
                                start=(hc == 0), stop=(hc == 63),
                                skip_group_check=True)
                            if hh == 15:
                                ins.then_inc(s_dnh, 1)
                    if fc >= 2:
                        stats_mm(sw, 1, fc - 2)
                stats_mm(sw, 1, NFC - 2)
                stats_mm(sw, 1, NFC - 1)
                ab_mm(sw, 1)
        blk.tensor(tensor_body)
    return nc



def kernel(L0_attn_hidden_states, L0_attn_Wq, L0_attn_Wk, L0_attn_Wv, L0_attn_Wo,
           L0_ffn_w_gate, L0_ffn_w_up, L0_ffn_w_down):
    x = np.ascontiguousarray(L0_attn_hidden_states)
    x_fm = np.ascontiguousarray(x.transpose(0, 2, 1))
    gmask, cmask, indsel, ident = _host_consts()
    perm = np.concatenate([np.arange((4 * r + hp) * 128, (4 * r + hp) * 128 + 128)
                           for hp in range(4) for r in range(4)])
    wo_perm = np.ascontiguousarray(L0_attn_Wo[perm])
    wgT = np.ascontiguousarray(L0_ffn_w_gate.T)
    wuT = np.ascontiguousarray(L0_ffn_w_up.T)
    wdT = np.ascontiguousarray(L0_ffn_w_down.T)

    nc = build_fused()
    in_maps = []
    for c in range(8):
        b, hg = c // 4, c % 4
        in_maps.append({
            "x_fm": x_fm[b],
            "wq_s": np.ascontiguousarray(L0_attn_Wq[:, hg * 512:(hg + 1) * 512]),
            "wk_s": np.ascontiguousarray(L0_attn_Wk[:, hg * 512:(hg + 1) * 512]),
            "wv_s": np.ascontiguousarray(L0_attn_Wv[:, hg * 512:(hg + 1) * 512]),
            "gmask": gmask, "cmask": cmask, "indsel": indsel, "ident": ident,
            "x_sl": np.ascontiguousarray(x_fm[b][:, hg * 2048:(hg + 1) * 2048]),
            "wo": wo_perm, "wgT": wgT, "wuT": wuT, "wdT": wdT,
        })
    res = run_bass_kernel_spmd(nc, in_maps, core_ids=list(range(8)))
    out = np.empty((2, S, HID), np.float32)
    for c in range(8):
        b, qtr = c // 4, c % 4
        out[b, qtr * 2048:(qtr + 1) * 2048, :] = res.results[c]["out_fm"].T
    return out


# version 11
# speedup vs baseline: 1.0570x; 1.0570x over previous
from contextlib import ExitStack

import numpy as np
import ml_dtypes

import concourse.bass as bass
import concourse.mybir as mybir
from concourse.bass_utils import run_bass_kernel_spmd

F32 = mybir.dt.float32
BF16 = mybir.dt.bfloat16
AF = mybir.ActivationFunctionType
ALU = mybir.AluOpType
AX = mybir.AxisListType

D = 128
HID = 2048
FFN = 8192
S = 8192
NB = 128
NTM = 16
NFC = 16
NSW = 4
SM_SCALE = float(D ** -0.5)
ESHIFT = -12.0
BIG = 1.0e30
NWC = 2 * (4 + 4 + 4 + 16)

bf16 = ml_dtypes.bfloat16


def _host_consts():
    qt = np.arange(64)[:, None, None]
    q = np.arange(128)[None, :, None]
    n = np.arange(NB)[None, None, :]
    qb = qt * 2 + q // 64
    gmask = np.where(n == qb, BIG, np.where(n > qb, -BIG, 0.0)).astype(np.float32)
    kk = np.arange(128)[:, None, None]
    o = np.arange(4)[None, :, None]
    u = np.arange(512)[None, None, :]
    cmask = np.where(128 * o + kk <= u, 0.0, -BIG).astype(np.float32).reshape(128, 4 * 512)
    j = np.arange(64)[:, None, None]
    n2 = np.arange(NB)[None, :, None]
    kk2 = np.arange(128)[None, None, :]
    indsel = np.where(n2 == 2 * j + kk2 // 64, BIG, 0.0).astype(bf16)
    ident = np.eye(128, dtype=np.float32)
    return gmask, cmask, indsel, ident


def build_fused():
    nc = bass.Bass("TRN2", target_bir_lowering=False)
    x_fm = nc.dram_tensor("x_fm", [HID, S], F32, kind="ExternalInput").ap()
    wq_s = nc.dram_tensor("wq_s", [HID, 512], F32, kind="ExternalInput").ap()
    wk_s = nc.dram_tensor("wk_s", [HID, 512], F32, kind="ExternalInput").ap()
    wv_s = nc.dram_tensor("wv_s", [HID, 512], F32, kind="ExternalInput").ap()
    gmask_d = nc.dram_tensor("gmask", [64, 128, 128], F32, kind="ExternalInput").ap()
    cmask_d = nc.dram_tensor("cmask", [128, 2048], F32, kind="ExternalInput").ap()
    indsel_d = nc.dram_tensor("indsel", [64, 128, 128], BF16, kind="ExternalInput").ap()
    ident_d = nc.dram_tensor("ident", [128, 128], F32, kind="ExternalInput").ap()
    x_sl = nc.dram_tensor("x_sl", [HID, 2048], F32, kind="ExternalInput").ap()
    wo_d = nc.dram_tensor("wo", [HID, HID], F32, kind="ExternalInput").ap()
    wgT_d = nc.dram_tensor("wgT", [HID, FFN], F32, kind="ExternalInput").ap()
    wuT_d = nc.dram_tensor("wuT", [HID, FFN], F32, kind="ExternalInput").ap()
    wdT_d = nc.dram_tensor("wdT", [FFN, HID], F32, kind="ExternalInput").ap()
    o_loc = nc.dram_tensor("o_loc", [512, S], BF16, kind="Internal").ap()
    x_bf = nc.dram_tensor("x_bf", [128, NFC * S], BF16, kind="Internal").ap()
    o_ag = nc.dram_tensor("o_ag", [HID, S], BF16, kind="Internal").ap()
    wo_bf = nc.dram_tensor("wo_bf", [HID, HID], BF16, kind="Internal").ap()
    wgT_bf = nc.dram_tensor("wgT_bf", [HID, FFN], BF16, kind="Internal").ap()
    wuT_bf = nc.dram_tensor("wuT_bf", [HID, FFN], BF16, kind="Internal").ap()
    wdT_bf = nc.dram_tensor("wdT_bf", [FFN, HID], BF16, kind="Internal").ap()
    out_d = nc.dram_tensor("out_fm", [HID, 2048], F32, kind="ExternalOutput").ap()

    es = ExitStack()
    Sb = es.enter_context
    with es:
        ARENA_COLS = 104448
        arena_t = Sb(nc.sbuf_tensor("arena", [128, ARENA_COLS], BF16))
        arena = arena_t[:]
        _off = [0]

        def take(shape, dtype):
            elems = 1
            for sdim in shape[1:]:
                elems *= sdim
            units = elems * (2 if dtype == F32 else 1)
            off = (_off[0] + 15) & ~15
            _off[0] = off + units
            assert _off[0] <= ARENA_COLS, f"arena overflow at {_off[0]}"
            ap = arena[:, off:off + units]
            if dtype == F32:
                ap = ap.bitcast(F32)
            if len(shape) == 3:
                ap = ap.rearrange("p (a b) -> p a b", a=shape[1])
            if shape[0] == 1:
                ap = ap[0:1]
            return ap

        ones_bf = take([128, 16], BF16)[:, 0:1]
        ones_f = take([1, 128], F32)
        nones_f = take([1, 128], F32)
        consts = take([128, 2], F32)
        epsc = take([1, 16], F32)[0:1, 0:1]
        ident_sb = take([128, 128], F32)
        PH_BASE = _off[0]

        xbf = take([128, 4, NFC * 512], BF16)
        q_buf = take([128, S], BF16)
        k_buf = take([128, S], BF16)
        v_buf = take([128, 64 * 132], BF16)
        wq_sb = take([128, NFC * 128], BF16)
        wk_sb = take([128, NFC * 128], BF16)
        wv_sb = take([128, NFC * 128], BF16)
        kg_sum = take([128, NB], F32)
        kg_bf = take([128, NB], BF16)
        gm_sb = take([128, 512], F32)
        top8 = take([128, 32], F32)
        gb_sb = take([128, 512], F32)
        gbT_buf = take([128, S], BF16)
        gmask_all = take([128, 16 * 512], BF16)
        gm_stg = take([128, 2, 512], F32)
        cmask_sb = take([128, 2048], F32)
        ind_sb = take([128, 64 * 128], BF16)
        pT = take([128, 4, 1024], BF16)
        o_tok = take([128, 2, 512], BF16)
        o_sb1 = take([128, 2, 512], BF16)
        rcpcol = take([128, 8], F32)
        ident_bf = take([128, 128], BF16)
        ph1_end = _off[0]

        _off[0] = PH_BASE
        o_sb = take([128, NFC * 512], BF16)
        y_sb = take([128, NFC * 512], F32)
        y1_sb = take([128, NFC * 512], F32)
        y2_sb = y_sb
        xbf_sb = take([128, NFC * 512], BF16)
        h_sb = take([128, 64 * 512], BF16)
        ybf_db = take([128, 2, 512], BF16)
        ysq_db = take([128, 2, 512], BF16)
        xch_db = take([128, 2, 512], F32)
        wo_db = take([128, 2, NFC * 128], BF16)
        wg_db = take([128, 2, 8 * 128], BF16)
        wu_db = take([128, 2, 8 * 128], BF16)
        wd_db = take([128, 2, 16 * 128], BF16)
        out_db = take([128, 2, 512], F32)
        mrow = take([1, 512], F32)
        rows2 = take([1, 1024], F32)
        ph2_end = _off[0]

        P01 = Sb(nc.psum_tensor("P01", [128, 1024], F32))
        P23 = Sb(nc.psum_tensor("P23", [128, 1024], F32))
        P4 = Sb(nc.psum_tensor("P4", [128, 512], F32))
        P5 = Sb(nc.psum_tensor("P5", [128, 512], F32))
        P6 = Sb(nc.psum_tensor("P6", [128, 512], F32))
        P7 = Sb(nc.psum_tensor("P7", [128, 512], F32))

        s_init = Sb(nc.semaphore("s_init"))
        s_cst = Sb(nc.semaphore("s_cst"))
        s_w = Sb(nc.semaphore("s_w"))
        s_x = [Sb(nc.semaphore(f"s_x{j}")) for j in range(4)]
        s_xw = [Sb(nc.semaphore(f"s_xw{j}")) for j in range(4)]
        s_x2 = [Sb(nc.semaphore(f"s_x2{j}")) for j in range(4)]
        s_proj = Sb(nc.semaphore("s_proj"))
        s_evq = Sb(nc.semaphore("s_evq"))
        s_evv = Sb(nc.semaphore("s_evv"))
        s_kg = Sb(nc.semaphore("s_kg"))
        s_gmld = [Sb(nc.semaphore("s_gmld0")), Sb(nc.semaphore("s_gmld1"))]
        s_gmcp = Sb(nc.semaphore("s_gmcp"))
        s_gmm = Sb(nc.semaphore("s_gmm"))
        s_gdve = Sb(nc.semaphore("s_gdve"))
        s_gtr = Sb(nc.semaphore("s_gtr"))
        s_gbT = Sb(nc.semaphore("s_gbT"))
        s_qk = Sb(nc.semaphore("s_qk"))
        s_msk = Sb(nc.semaphore("s_msk"))
        s_exp = Sb(nc.semaphore("s_exp"))
        s_pv = Sb(nc.semaphore("s_pv"))
        s_vm = Sb(nc.semaphore("s_vm"))
        s_onm = Sb(nc.semaphore("s_onm"))
        s_otr = Sb(nc.semaphore("s_otr"))
        s_om = Sb(nc.semaphore("s_om"))
        s_st = [Sb(nc.semaphore("s_st0")), Sb(nc.semaphore("s_st1"))]
        s_cc = Sb(nc.semaphore("s_cc"))
        s_wc_o = Sb(nc.semaphore("s_wc_o"))
        s_wc_gu = Sb(nc.semaphore("s_wc_gu"))
        s_wc_d = Sb(nc.semaphore("s_wc_d"))
        s_ld = Sb(nc.semaphore("s_ld"))
        s_xc = [Sb(nc.semaphore("s_xc0")), Sb(nc.semaphore("s_xc1"))]
        s_wo = [Sb(nc.semaphore("s_wo0")), Sb(nc.semaphore("s_wo1"))]
        s_wg = [Sb(nc.semaphore("s_wg0")), Sb(nc.semaphore("s_wg1"))]
        s_wd = [Sb(nc.semaphore("s_wd0")), Sb(nc.semaphore("s_wd1"))]
        s_at = Sb(nc.semaphore("s_at"))
        s_y = Sb(nc.semaphore("s_y"))
        s_yb = Sb(nc.semaphore("s_yb"))
        s_sm = Sb(nc.semaphore("s_sm"))
        s_ln = Sb(nc.semaphore("s_ln"))
        s_sq = Sb(nc.semaphore("s_sq"))
        s_ab = Sb(nc.semaphore("s_ab"))
        s_n1 = Sb(nc.semaphore("s_n1"))
        s_n2 = Sb(nc.semaphore("s_n2"))
        s_xb = Sb(nc.semaphore("s_xb"))
        s_si = Sb(nc.semaphore("s_si"))
        s_h = Sb(nc.semaphore("s_h"))
        s_dnh = Sb(nc.semaphore("s_dnh"))
        s_guh = Sb(nc.semaphore("s_guh"))
        s_y2 = Sb(nc.semaphore("s_y2"))
        t_st = [Sb(nc.semaphore("t_st0")), Sb(nc.semaphore("t_st1"))]

        blk = Sb(nc.Block())

        def grp_count(qm):
            return 2 * (qm + 1)

        def gidx(hp, qm, jg):
            return hp * 272 + qm * (qm + 1) + jg

        def qm_idx(hp, qm):
            return hp * 16 + qm

        wcast_chunks = []
        for (dst, src, rows_, cols_, csem) in ((wo_bf, wo_d, HID, HID, s_wc_o),
                                               (wgT_bf, wgT_d, HID, FFN, s_wc_gu),
                                               (wuT_bf, wuT_d, HID, FFN, s_wc_gu),
                                               (wdT_bf, wdT_d, FFN, HID, s_wc_d)):
            for r0 in range(0, rows_, 256):
                wcast_chunks.append((dst, src, r0, 256 * cols_ * 4, csem))
        assert len(wcast_chunks) == NWC

        def _prog(om):
            return (om // 16) * 136 + (om % 16) * (om % 16 + 1) // 2

        _tot_b = sum(c[3] for c in wcast_chunks)
        _pend = _prog(46)
        wcast_tgt = []
        _acc = 0
        for c in wcast_chunks:
            t = 1
            while _prog(t) < _acc / _tot_b * _pend:
                t += 1
            wcast_tgt.append(t)
            _acc += c[3]
        assert max(wcast_tgt) <= 46

        def sync_body(sync):
            sync.dma_start(cmask_sb, cmask_d).then_inc(s_cst, 16)
            sync.dma_start(ind_sb.rearrange("p (j k) -> p j k", j=64),
                           indsel_d.rearrange("j p k -> p j k")).then_inc(s_cst, 16)
            sync.dma_start(ident_sb, ident_d).then_inc(s_cst, 16)
            for hp in range(4):
                if hp == 0:
                    for tm in range(NTM):
                        sync.wait_ge(s_x[tm % 4], 16 * (tm // 4 + 1))
                        sync.dma_start(x_bf[:, tm * 512 * NFC:(tm + 1) * 512 * NFC],
                                       xbf[:, tm % 4, :]).then_inc(s_xw[tm % 4], 16)
                for qm in range(16):
                    qi = qm_idx(hp, qm)
                    sync.wait_ge(s_om, qi + 1)
                    sync.dma_start(o_loc[hp * 128:hp * 128 + 128, qm * 512:qm * 512 + 512],
                                   o_sb1[:, qm % 2, :]).then_inc(s_st[qi % 2], 16)
                    if hp < 3 and qm < 4:
                        i2 = (hp + 1) * 16 + qm
                        if qm == 0 and hp == 0:
                            for j in range(4):
                                sync.wait_ge(s_xw[j], 16 * 4)
                        sync.wait_ge(s_proj, i2 - 3)
                        sync.dma_start(xbf[:, i2 % 4, :],
                                       x_bf[:, qm * 512 * NFC:(qm + 1) * 512 * NFC]
                                       ).then_inc(s_x2[i2 % 4], 16)
                if hp < 3:
                    for qm in range(4, 16):
                        i2 = (hp + 1) * 16 + qm
                        sync.wait_ge(s_proj, i2 - 3)
                        sync.dma_start(xbf[:, i2 % 4, :],
                                       x_bf[:, qm * 512 * NFC:(qm + 1) * 512 * NFC]
                                       ).then_inc(s_x2[i2 % 4], 16)
            sync.wait_ge(s_st[0], 16 * 32)
            sync.wait_ge(s_st[1], 16 * 32)
            sync.wait_ge(s_wc_o, 16 * 8)
            for sw in range(NSW):
                def emit_xw(sw2, fc):
                    fi2 = sw2 * NFC + fc
                    if fi2 >= 2:
                        sync.wait_ge(s_y, fi2 - 1)
                        sync.wait_ge(s_at, fi2 - 1)
                    sync.dma_start(
                        xch_db[:, fi2 % 2, :],
                        x_sl[fc * 128:fc * 128 + 128, sw2 * 512:sw2 * 512 + 512],
                    ).then_inc(s_xc[fi2 % 2], 16)
                    sync.dma_start(
                        wo_db[:, fi2 % 2, :].rearrange("p (c d) -> p c d", c=NFC),
                        wo_bf.rearrange("(c p) f -> p c f", p=128)[:, :, fc * 128:fc * 128 + 128],
                    ).then_inc(s_wo[fi2 % 2], 16)

                for fc in range(NFC):
                    if sw >= 1 and fc < 2:
                        continue
                    emit_xw(sw, fc)
                for hc in range(64):
                    for half in range(2):
                        hi = sw * 64 + hc
                        wgi = hi * 2 + half
                        if wgi == 0:
                            sync.wait_ge(s_wc_gu, 16 * 16)
                        if wgi >= 2:
                            sync.wait_ge(s_guh, wgi - 1)
                        sync.dma_start(
                            wg_db[:, wgi % 2, :].rearrange("p (c d) -> p c d", c=8),
                            wgT_bf.rearrange("(c p) f -> p c f", p=128)[
                                :, half * 8:half * 8 + 8, hc * 128:hc * 128 + 128],
                        ).then_inc(s_wg[wgi % 2], 16)
                        sync.dma_start(
                            wu_db[:, wgi % 2, :].rearrange("p (c d) -> p c d", c=8),
                            wuT_bf.rearrange("(c p) f -> p c f", p=128)[
                                :, half * 8:half * 8 + 8, hc * 128:hc * 128 + 128],
                        ).then_inc(s_wg[wgi % 2], 16)
                if sw < NSW - 1:
                    emit_xw(sw + 1, 0)
                    emit_xw(sw + 1, 1)
                for fc in range(NFC):
                    for q4 in range(4):
                        fi = sw * NFC + fc
                        wdi = fi * 4 + q4
                        if wdi == 0:
                            sync.wait_ge(s_wc_d, 16 * 32)
                        if wdi >= 2:
                            sync.wait_ge(s_dnh, wdi - 1)
                        sync.dma_start(
                            wd_db[:, wdi % 2, :].rearrange("p (c d) -> p c d", c=16),
                            wdT_bf.rearrange("(c p) f -> p c f", p=128)[
                                :, q4 * 16:q4 * 16 + 16, fc * 128:fc * 128 + 128],
                        ).then_inc(s_wd[wdi % 2], 16)
        blk.sync(sync_body)

        def gpsimd_body(gpsimd):
            wci = 0

            def head_loads(hp):
                nonlocal wci
                if hp >= 1:
                    gpsimd.wait_ge(s_proj, hp * 16)
                for wdst, wsrc in ((wq_sb, wq_s), (wk_sb, wk_s), (wv_sb, wv_s)):
                    gpsimd.dma_start(
                        wdst.rearrange("p (c d) -> p c d", c=NFC),
                        wsrc.rearrange("(c p) hd -> p c hd", p=128)[:, :, hp * 128:hp * 128 + 128],
                    ).then_inc(s_w, 16)
                if hp >= 1:
                    return
                for tm in range(NTM):
                    i = hp * 16 + tm
                    if i >= 4:
                        gpsimd.wait_ge(s_proj, i - 3)
                        gpsimd.wait_ge(s_xw[i % 4], 16 * (i // 4))
                    gpsimd.dma_start(
                        xbf[:, i % 4, :].rearrange("p (c n) -> p c n", c=NFC),
                        x_fm.rearrange("(c p) s -> p c s", p=128)[:, :, tm * 512:tm * 512 + 512],
                    ).then_inc(s_x[i % 4], 16)


            def head_ag(hp, half=None):
                if half is None:
                    cols = slice(0, S)
                    st_need = 16 * 8 * (hp + 1)
                else:
                    cols = slice(half * 4096, half * 4096 + 4096)
                    st_need = 16 * (8 * hp + 4 * (half + 1))
                gpsimd.wait_ge(s_st[0], st_need)
                gpsimd.wait_ge(s_st[1], st_need)
                gpsimd.collective_compute(
                    "AllGather", ALU.bypass,
                    replica_groups=[[0, 1, 2, 3], [4, 5, 6, 7]],
                    ins=[o_loc[hp * 128:hp * 128 + 128, cols].opt()],
                    outs=[o_ag[hp * 512:hp * 512 + 512, cols].opt()],
                ).then_inc(s_cc, 1)

            def wcast_until(om_hi):
                nonlocal wci
                while wci < NWC and wcast_tgt[wci] < om_hi:
                    gpsimd.wait_ge(s_om, wcast_tgt[wci])
                    dst, src, r0, _, csem = wcast_chunks[wci]
                    gpsimd.dma_start(dst[r0:r0 + 256, :],
                                     src[r0:r0 + 256, :]).then_inc(csem, 16)
                    wci += 1

            head_loads(0)
            head_loads(1)
            wcast_until(16)
            head_ag(0)
            head_loads(2)
            wcast_until(32)
            head_ag(1)
            head_loads(3)
            wcast_until(47)
            head_ag(2)
            wcast_until(65)
            qtr = gpsimd.partition_id() % 4
            o_ag_r = o_ag.rearrange("(c p) n -> p c n", p=128)
            o_sb_r = o_sb.rearrange("p (c n) -> p c n", c=NFC)
            base = o_ag_r[:, :, bass.ts(qtr, 2048)]
            gpsimd.wait_ge(s_proj, 61)
            for hp4 in range(3):
                gpsimd.wait_ge(s_cc, hp4 + 1)
                gpsimd.dma_start(o_sb_r[:, 4 * hp4:4 * hp4 + 4, :],
                                 base[:, 4 * hp4:4 * hp4 + 4, 0:512]
                                 ).then_inc(s_ld, 16)
            head_ag(3)
            gpsimd.wait_ge(s_cc, 4)
            gpsimd.dma_start(o_sb_r[:, 12:16, :],
                             base[:, 12:16, 0:512]).then_inc(s_ld, 16)
            for sw in range(1, NSW):
                gpsimd.wait_ge(s_at, sw * NFC)
                gpsimd.dma_start(o_sb_r,
                                 base[:, :, sw * 512:sw * 512 + 512]
                                 ).then_inc(s_ld, 16)
        blk.gpsimd(gpsimd_body)

        def vector_body(vector):
            vector.memset(ones_bf, 1.0)
            vector.memset(ones_f, 1.0)
            vector.memset(nones_f, -1.0)
            vector.memset(epsc, 1e-5)
            vector.memset(consts[:, 0:1], ESHIFT)
            vector.memset(consts[:, 1:2], SM_SCALE)
            vector.wait_ge(s_cst, 48)
            vector.tensor_copy(ident_bf, ident_sb).then_inc(s_init, 1)
            for i in range(16):
                vector.wait_ge(s_gmld[i % 2], 16 * (i // 2 + 1))
                vector.tensor_copy(gmask_all[:, i * 512:i * 512 + 512],
                                   gm_stg[:, i % 2, :]).then_inc(s_gmcp, 1)
            for hp in range(4):
                if hp >= 1:
                    vector.wait_ge(s_om, hp * 16)
                vector.memset(v_buf.rearrange("p (t c) -> p t c", t=64)[:, :, 128:129],
                              1.0).then_inc(s_vm, 1)
                for tm in range(NTM):
                    i = hp * 16 + tm
                    vector.wait_ge(s_proj, i + 1)
                    qsrc = P01 if tm % 2 == 0 else P23
                    vector.tensor_copy(q_buf[:, tm * 512:tm * 512 + 512], qsrc[:, 0:512])
                    vector.tensor_copy(k_buf[:, tm * 512:tm * 512 + 512],
                                       qsrc[:, 512:1024]).then_inc(s_evq, 1)
                vector.wait_ge(s_evq, (hp + 1) * 16)
                vector.tensor_reduce(kg_sum, k_buf.rearrange("p (a b) -> p a b", b=64),
                                     AX.X, ALU.add).then_inc(s_kg, 1)
                vector.wait_ge(s_kg, 2 * hp + 1)
                vector.tensor_copy(kg_bf, kg_sum).then_inc(s_kg, 1)
                for b4 in range(16):
                    i = hp * 16 + b4
                    vector.wait_ge(s_gmm, i + 1)
                    if hp == 0:
                        vector.wait_ge(s_gmcp, b4 + 1)
                    gsl = (P4 if b4 % 2 == 0 else P5)[:]
                    vector.tensor_tensor(gm_sb, gsl,
                                         gmask_all[:, b4 * 512:b4 * 512 + 512],
                                         ALU.add).then_inc(s_gdve, 1)
                    for c in range(4):
                        if c == 0:
                            vector.wait_ge(s_gdve, 9 * i + 1)
                        vector.max(top8[:, c * 8:c * 8 + 8],
                                   gm_sb[:, c * 128:c * 128 + 128]).then_inc(s_gdve, 1)
                    for c in range(4):
                        vector.wait_ge(s_gdve, 9 * i + 2 + c)
                        vector.tensor_scalar(gb_sb[:, c * 128:c * 128 + 128],
                                             gm_sb[:, c * 128:c * 128 + 128],
                                             top8[:, c * 8 + 7:c * 8 + 8], 1.0,
                                             ALU.is_ge, ALU.subtract).then_inc(s_gdve, 1)
                    vector.wait_ge(s_gtr, 4 * i + 4)
                    tsl = (P6 if b4 % 2 == 0 else P7)[:]
                    vector.tensor_copy(gbT_buf[:, b4 * 512:b4 * 512 + 512],
                                       tsl).then_inc(s_gbT, 1)
                for qm in range(16):
                    qi = qm_idx(hp, qm)
                    ng = grp_count(qm)
                    for jg in (ng - 2, ng - 1):
                        g = gidx(hp, qm, jg)
                        vector.wait_ge(s_qk, g + 1)
                        sc = P01 if jg % 2 == 0 else P23
                        for t in range(2):
                            off = (2 * jg + t) - 4 * qm
                            ins = vector.tensor_tensor(
                                sc[:, t * 512:t * 512 + 512], sc[:, t * 512:t * 512 + 512],
                                cmask_sb[:, off * 512:off * 512 + 512], ALU.add)
                            if t == 1:
                                ins.then_inc(s_msk, 1)
                    vector.wait_ge(s_pv, gidx(hp, qm, ng - 1) + 1)
                    OACC = (P4, P5, P6, P7)
                    for c in range(4):
                        vector.reciprocal(rcpcol[:, c:c + 1],
                                          OACC[c][:, 128:129]).then_inc(s_onm, 1)
                    for c in range(4):
                        vector.wait_ge(s_onm, 8 * qi + c + 1)
                        vector.tensor_scalar(o_tok[:, qm % 2, c * 128:c * 128 + 128],
                                             OACC[c][:, 0:128], rcpcol[:, c:c + 1], None,
                                             ALU.mult).then_inc(s_onm, 1)
                    for c in range(4):
                        vector.wait_ge(s_otr, 4 * qi + c + 1)
                        if c == 0 and qi >= 2:
                            vector.wait_ge(s_st[qi % 2], 16 * (qi // 2))
                        obt = OACC[c][:, 256:512].bitcast(BF16)[:, 0:128]
                        ins = vector.tensor_copy(o_sb1[:, qm % 2, c * 128:c * 128 + 128], obt)
                        if c == 3:
                            ins.then_inc(s_om, 1)

            def ln_rows(li):
                vector.wait_ge(s_sm, (li + 1) * NFC)
                if li >= 1:
                    vector.wait_ge(s_ab, 2 * li)
                vector.tensor_scalar(mrow, P6[0:1, 0:512], 1.0 / 2048.0, None,
                                     ALU.mult).then_inc(s_ln, 1)
                vector.wait_ge(s_ln, li * 6 + 1)
                vector.tensor_scalar(rows2[0:1, 0:512], P7[0:1, 0:512], 1.0 / 2048.0, None,
                                     ALU.mult).then_inc(s_ln, 1)
                vector.wait_ge(s_ln, li * 6 + 2)
                vector.tensor_tensor(rows2[0:1, 512:1024], mrow, mrow,
                                     ALU.mult).then_inc(s_ln, 1)
                vector.wait_ge(s_ln, li * 6 + 3)
                vector.tensor_tensor(rows2[0:1, 0:512], rows2[0:1, 0:512],
                                     rows2[0:1, 512:1024], ALU.subtract).then_inc(s_ln, 1)
                vector.wait_ge(s_sq, li + 1)
                vector.reciprocal(rows2[0:1, 0:512], rows2[0:1, 512:1024]).then_inc(s_ln, 1)
                vector.wait_ge(s_ln, li * 6 + 5)
                vector.tensor_tensor(rows2[0:1, 512:1024], mrow, rows2[0:1, 0:512],
                                     ALU.mult).then_inc(s_ln, 1)

            for sw in range(NSW):
                for fc in range(NFC):
                    fi = sw * NFC + fc
                    vector.wait_ge(s_at, fi + 1)
                    vector.wait_ge(s_xc[fi % 2], 16 * (fi // 2 + 1))
                    asrc = P4[:] if fc % 2 == 0 else P5[:]
                    vector.tensor_tensor(y_sb[:, fc * 512:fc * 512 + 512], asrc,
                                         xch_db[:, fi % 2, :], ALU.add).then_inc(s_y, 1)
                ln_rows(2 * sw)
                for fc in range(NFC):
                    li = 2 * sw
                    gi = li * NFC + fc
                    vector.wait_ge(s_ab, 2 * li + 2)
                    vector.tensor_tensor(y1_sb[:, fc * 512:fc * 512 + 512],
                                         y_sb[:, fc * 512:fc * 512 + 512],
                                         P01[:, 0:512], ALU.mult).then_inc(s_n1, 1)
                    vector.wait_ge(s_n1, gi + 1)
                    vector.tensor_tensor(y1_sb[:, fc * 512:fc * 512 + 512],
                                         y1_sb[:, fc * 512:fc * 512 + 512],
                                         P01[:, 512:1024], ALU.add).then_inc(s_n2, 1)
                for hc in range(64):
                    hi = sw * 64 + hc
                    vector.wait_ge(s_si, hi + 1)
                    usrc = P23[:, 512:1024] if hc % 2 == 0 else P5[:]
                    vector.tensor_tensor(h_sb[:, hc * 512:hc * 512 + 512],
                                         h_sb[:, hc * 512:hc * 512 + 512], usrc,
                                         ALU.mult).then_inc(s_h, 1)
                for fc in range(NFC):
                    fi = sw * NFC + fc
                    vector.wait_ge(s_dnh, 4 * (fi + 1))
                    fsrc = P23[:, 0:512] if fc % 2 == 0 else P23[:, 512:1024]
                    vector.tensor_tensor(y2_sb[:, fc * 512:fc * 512 + 512], fsrc,
                                         y1_sb[:, fc * 512:fc * 512 + 512],
                                         ALU.add).then_inc(s_y2, 1)
                ln_rows(2 * sw + 1)
                for fc in range(NFC):
                    li = 2 * sw + 1
                    gi = li * NFC + fc
                    fi = sw * NFC + fc
                    vector.wait_ge(s_ab, 2 * li + 2)
                    if fi >= 2:
                        vector.wait_ge(t_st[fi % 2], 16 * (fi // 2))
                    vector.tensor_tensor(out_db[:, fi % 2, :],
                                         y2_sb[:, fc * 512:fc * 512 + 512],
                                         P01[:, 0:512], ALU.mult).then_inc(s_n1, 1)
                    vector.wait_ge(s_n1, gi + 1)
                    vector.tensor_tensor(out_db[:, fi % 2, :], out_db[:, fi % 2, :],
                                         P01[:, 512:1024], ALU.add).then_inc(s_n2, 1)
        blk.vector(vector_body)

        def scalar_body(scalar):
            for i in range(16):
                if i >= 2:
                    scalar.wait_ge(s_gmcp, i - 1)
                scalar.dma_start(
                    gm_stg[:, i % 2, :].rearrange("p (q n) -> p q n", q=4),
                    gmask_d.rearrange("q p n -> p q n")[:, i * 4:i * 4 + 4, :],
                ).then_inc(s_gmld[i % 2], 16)
            scalar.wait_ge(s_init, 1)
            for hp in range(4):
                for tm in range(NTM):
                    i = hp * 16 + tm
                    scalar.wait_ge(s_proj, i + 1)
                    vsrc = P4 if tm % 2 == 0 else P5
                    vdst3 = v_buf[:, tm * 528:tm * 528 + 528].rearrange(
                        "p (t c) -> p t c", t=4)[:, :, 0:128]
                    scalar.copy(vdst3, vsrc[:].rearrange("p (t c) -> p t c", t=4)
                                ).then_inc(s_evv, 1)
                for qm in range(16):
                    qi = qm_idx(hp, qm)
                    ng = grp_count(qm)
                    for jg in range(ng):
                        g = gidx(hp, qm, jg)
                        if jg >= ng - 2:
                            scalar.wait_ge(s_msk, hp * 32 + qm * 2 + (jg - (ng - 2)) + 1)
                        else:
                            scalar.wait_ge(s_qk, g + 1)
                        sc = P01 if jg % 2 == 0 else P23
                        if g >= 4:
                            scalar.wait_ge(s_pv, g - 3)
                        scalar.activation(pT[:, g % 4, :], sc[:], AF.Exp,
                                          bias=consts[:, 0:1],
                                          scale=consts[:, 1:2]).then_inc(s_exp, 1)

            def sqrt_row(li):
                scalar.wait_ge(s_ln, li * 6 + 4)
                scalar.activation(rows2[0:1, 512:1024], rows2[0:1, 0:512], AF.Sqrt,
                                  bias=epsc).then_inc(s_sq, 1)

            for sw in range(NSW):
                for fc in range(NFC):
                    li = 2 * sw
                    gi = li * NFC + fc
                    scalar.wait_ge(s_y, sw * NFC + fc + 1)
                    if gi >= 2:
                        scalar.wait_ge(s_sm, gi - 1)
                    scalar.activation(ybf_db[:, gi % 2, :], y_sb[:, fc * 512:fc * 512 + 512],
                                      AF.Identity).then_inc(s_yb, 1)
                    scalar.activation(ysq_db[:, gi % 2, :], y_sb[:, fc * 512:fc * 512 + 512],
                                      AF.Square).then_inc(s_yb, 1)
                sqrt_row(2 * sw)
                for fc in range(NFC):
                    gi = 2 * sw * NFC + fc
                    scalar.wait_ge(s_n2, gi + 1)
                    scalar.activation(xbf_sb[:, fc * 512:fc * 512 + 512],
                                      y1_sb[:, fc * 512:fc * 512 + 512],
                                      AF.Identity).then_inc(s_xb, 1)
                for hc in range(64):
                    hi = sw * 64 + hc
                    scalar.wait_ge(s_guh, 2 * (hi + 1))
                    gsrc = P23[:, 0:512] if hc % 2 == 0 else P4[:]
                    scalar.activation(h_sb[:, hc * 512:hc * 512 + 512], gsrc,
                                      AF.Silu).then_inc(s_si, 1)
                for fc in range(NFC):
                    li = 2 * sw + 1
                    gi = li * NFC + fc
                    scalar.wait_ge(s_y2, sw * NFC + fc + 1)
                    if gi >= 2:
                        scalar.wait_ge(s_sm, gi - 1)
                    scalar.activation(ybf_db[:, gi % 2, :], y2_sb[:, fc * 512:fc * 512 + 512],
                                      AF.Identity).then_inc(s_yb, 1)
                    scalar.activation(ysq_db[:, gi % 2, :], y2_sb[:, fc * 512:fc * 512 + 512],
                                      AF.Square).then_inc(s_yb, 1)
                sqrt_row(2 * sw + 1)
                for fc in range(NFC):
                    li = 2 * sw + 1
                    gi = li * NFC + fc
                    fi = sw * NFC + fc
                    scalar.wait_ge(s_n2, gi + 1)
                    scalar.dma_start(out_d[fc * 128:fc * 128 + 128, sw * 512:sw * 512 + 512],
                                     out_db[:, fi % 2, :]).then_inc(t_st[fi % 2], 16)
            scalar.wait_ge(t_st[0], 16 * NSW * NFC // 2)
            scalar.wait_ge(t_st[1], 16 * NSW * NFC // 2)
        blk.scalar(scalar_body)

        def tensor_body(tensor):
            tensor.wait_ge(s_init, 1)
            tensor.wait_ge(s_cst, 48)
            for hp in range(4):
                tensor.wait_ge(s_w, (hp * 3 + 3) * 16)
                for tm in range(NTM):
                    i = hp * 16 + tm
                    if hp == 0:
                        tensor.wait_ge(s_x[i % 4], 16 * (i // 4 + 1))
                    else:
                        tensor.wait_ge(s_x2[i % 4], 16 * ((i - 16) // 4 + 1))
                    if tm == 0 and hp >= 1:
                        tensor.wait_ge(s_om, hp * 16)
                    if i >= 2:
                        tensor.wait_ge(s_evq, i - 1)
                        tensor.wait_ge(s_evv, i - 1)
                    xt = xbf[:, i % 4, :]
                    qdst = P01 if tm % 2 == 0 else P23
                    vdst = P4 if tm % 2 == 0 else P5
                    for fc in range(NFC):
                        tensor.matmul(qdst[:, 0:512], wq_sb[:, fc * 128:fc * 128 + 128],
                                      xt[:, fc * 512:fc * 512 + 512],
                                      start=(fc == 0), stop=(fc == NFC - 1))
                        tensor.matmul(qdst[:, 512:1024], wk_sb[:, fc * 128:fc * 128 + 128],
                                      xt[:, fc * 512:fc * 512 + 512],
                                      start=(fc == 0), stop=(fc == NFC - 1))
                        for ts4 in range(4):
                            ins = tensor.matmul(
                                vdst[:, ts4 * 128:ts4 * 128 + 128],
                                xt[:, fc * 512 + ts4 * 128:fc * 512 + ts4 * 128 + 128],
                                wv_sb[:, fc * 128:fc * 128 + 128],
                                start=(fc == 0 and ts4 == 0),
                                stop=(fc == NFC - 1 and ts4 == 3))
                            if fc == NFC - 1 and ts4 == 3:
                                ins.then_inc(s_proj, 1)
                tensor.wait_ge(s_kg, 2 * (hp + 1))
                tensor.wait_ge(s_evq, (hp + 1) * 16)
                tensor.wait_ge(s_evv, (hp + 1) * 16)
                for b4 in range(16):
                    i = hp * 16 + b4
                    gdst = (P4 if b4 % 2 == 0 else P5)[:]
                    if i >= 2:
                        tensor.wait_ge(s_gdve, 9 * (i - 1))
                    for c in range(4):
                        ins = tensor.matmul(gdst[:, c * 128:c * 128 + 128],
                                            q_buf[:, (b4 * 4 + c) * 128:(b4 * 4 + c) * 128 + 128],
                                            kg_bf, start=(c == 0), stop=(c == 3),
                                            skip_group_check=True)
                        if c == 3:
                            ins.then_inc(s_gmm, 1)
                    tdst = (P6 if b4 % 2 == 0 else P7)[:]
                    if i >= 2:
                        tensor.wait_ge(s_gbT, i - 1)
                    for c in range(4):
                        tensor.wait_ge(s_gdve, 9 * i + 6 + c)
                        tensor.transpose(tdst[:, c * 128:c * 128 + 128],
                                         gb_sb[:, c * 128:c * 128 + 128],
                                         ident_sb).then_inc(s_gtr, 1)
                tensor.wait_ge(s_gbT, (hp + 1) * 16)
                tensor.wait_ge(s_vm, hp + 1)
                OACC = (P4, P5, P6, P7)

                def emit_pv(qm, jg_pv):
                    qi = qm_idx(hp, qm)
                    ng = grp_count(qm)
                    gp = gidx(hp, qm, jg_pv)
                    tensor.wait_ge(s_exp, gp + 1)
                    if jg_pv == 0 and qi > 0:
                        tensor.wait_ge(s_onm, 8 * qi)
                        tensor.wait_ge(s_om, qi)
                    pt = pT[:, gp % 4, :]
                    for tt in range(2):
                        jj = 2 * jg_pv + tt
                        last = (jg_pv == ng - 1 and tt == 1)
                        for c in range(4):
                            ins = tensor.matmul(
                                OACC[c][:, 0:129],
                                pt[:, tt * 512 + c * 128:tt * 512 + c * 128 + 128],
                                v_buf[:, jj * 132:jj * 132 + 129],
                                start=(jg_pv == 0 and tt == 0), stop=last,
                                skip_group_check=True)
                            if tt == 1 and c == 3:
                                ins.then_inc(s_pv, 1)

                def emit_otr(qm):
                    qi = qm_idx(hp, qm)
                    for c in range(4):
                        tensor.wait_ge(s_onm, 8 * qi + 4 + c + 1)
                        obt = OACC[c][:, 256:512].bitcast(BF16)[:, 0:128]
                        tensor.transpose(obt, o_tok[:, qm % 2, c * 128:c * 128 + 128],
                                         ident_bf).then_inc(s_otr, 1)

                for qm in range(16):
                    qi = qm_idx(hp, qm)
                    ng = grp_count(qm)
                    qrhs = q_buf[:, qm * 512:qm * 512 + 512]
                    for jg in range(ng):
                        g = gidx(hp, qm, jg)
                        sc = P01 if jg % 2 == 0 else P23
                        if g >= 2:
                            tensor.wait_ge(s_exp, g - 1)
                        for t in range(2):
                            j = 2 * jg + t
                            tensor.matmul(sc[:, t * 512:t * 512 + 512],
                                          k_buf[:, j * 128:j * 128 + 128], qrhs,
                                          start=True, stop=False)
                            ins = tensor.matmul(sc[:, t * 512:t * 512 + 512],
                                                ind_sb[:, j * 128:j * 128 + 128],
                                                gbT_buf[:, qm * 512:qm * 512 + 512],
                                                start=False, stop=True)
                            if t == 1:
                                ins.then_inc(s_qk, 1)
                        if jg == 1 and qm >= 1:
                            emit_otr(qm - 1)
                        if jg >= 2:
                            emit_pv(qm, jg - 2)
                        if jg == ng - 1:
                            emit_pv(qm, jg - 1)
                            emit_pv(qm, jg)
                    if qm == 15:
                        emit_otr(qm)

            def stats_mm(sw, ln, fc):
                li = 2 * sw + ln
                gi = li * NFC + fc
                tensor.wait_ge(s_yb, 2 * (gi + 1))
                if fc == 0 and li >= 1:
                    tensor.wait_ge(s_ln, (li - 1) * 6 + 2)
                tensor.matmul(P6[0:1, 0:512], ones_bf,
                              ybf_db[:, gi % 2, :],
                              start=(fc == 0), stop=(fc == NFC - 1), skip_group_check=True)
                tensor.matmul(P7[0:1, 0:512], ones_bf,
                              ysq_db[:, gi % 2, :],
                              start=(fc == 0), stop=(fc == NFC - 1),
                              skip_group_check=True).then_inc(s_sm, 1)

            def ab_mm(sw, ln):
                li = 2 * sw + ln
                tensor.wait_ge(s_ln, li * 6 + 5)
                if li >= 1:
                    tensor.wait_ge(s_n2, li * NFC)
                tensor.matmul(P01[:, 0:512], ones_f, rows2[0:1, 0:512],
                              start=True, stop=True)
                tensor.wait_ge(s_ln, li * 6 + 6)
                tensor.matmul(P01[:, 512:1024], nones_f, rows2[0:1, 512:1024],
                              start=True, stop=True).then_inc(s_ab, 2)

            for sw in range(NSW):
                tensor.wait_ge(s_ld, 64 + 16 * sw)
                for fc in range(NFC):
                    fi = sw * NFC + fc
                    tensor.wait_ge(s_wo[fi % 2], 16 * (fi // 2 + 1))
                    if fc == 0 and sw == 0:
                        tensor.wait_ge(s_om, 64)
                    if fc == 0 and sw >= 1:
                        tensor.wait_ge(s_h, sw * 64)
                    if fi >= 2:
                        tensor.wait_ge(s_y, fi - 1)
                    adst = P4[:] if fc % 2 == 0 else P5[:]
                    ws = wo_db[:, fi % 2, :]
                    for oc in range(NFC):
                        ins = tensor.matmul(adst, ws[:, oc * 128:oc * 128 + 128],
                                            o_sb[:, oc * 512:oc * 512 + 512],
                                            start=(oc == 0), stop=(oc == NFC - 1),
                                            skip_group_check=True)
                        if oc == NFC - 1:
                            ins.then_inc(s_at, 1)
                    if fc >= 1:
                        stats_mm(sw, 0, fc - 1)
                stats_mm(sw, 0, NFC - 1)
                ab_mm(sw, 0)
                for hc in range(64):
                    hi = sw * 64 + hc
                    if hc == 0 and sw >= 1:
                        tensor.wait_ge(s_y2, sw * NFC)
                    if hi >= 2:
                        tensor.wait_ge(s_h, hi - 1)
                    gdst = P23[:, 0:512] if hc % 2 == 0 else P4[:]
                    udst = P23[:, 512:1024] if hc % 2 == 0 else P5[:]
                    for half in range(2):
                        wgi = hi * 2 + half
                        tensor.wait_ge(s_wg[wgi % 2], 32 * (wgi // 2 + 1))
                        for ff in range(8):
                            fc = half * 8 + ff
                            if hc == 0:
                                tensor.wait_ge(s_xb, sw * NFC + fc + 1)
                            tensor.matmul(gdst, wg_db[:, wgi % 2, ff * 128:ff * 128 + 128],
                                          xbf_sb[:, fc * 512:fc * 512 + 512],
                                          start=(fc == 0), stop=(fc == NFC - 1),
                                          skip_group_check=True)
                            ins = tensor.matmul(udst, wu_db[:, wgi % 2, ff * 128:ff * 128 + 128],
                                                xbf_sb[:, fc * 512:fc * 512 + 512],
                                                start=(fc == 0), stop=(fc == NFC - 1),
                                                skip_group_check=True)
                            if ff == 7:
                                ins.then_inc(s_guh, 1)
                for fc in range(NFC):
                    fi = sw * NFC + fc
                    if fc == 0:
                        tensor.wait_ge(s_h, (sw + 1) * 64)
                    if fi >= 2:
                        tensor.wait_ge(s_y2, fi - 1)
                    fdst = P23[:, 0:512] if fc % 2 == 0 else P23[:, 512:1024]
                    for q4 in range(4):
                        wdi = fi * 4 + q4
                        tensor.wait_ge(s_wd[wdi % 2], 16 * (wdi // 2 + 1))
                        for hh in range(16):
                            hc = q4 * 16 + hh
                            ins = tensor.matmul(
                                fdst, wd_db[:, wdi % 2, hh * 128:hh * 128 + 128],
                                h_sb[:, hc * 512:hc * 512 + 512],
                                start=(hc == 0), stop=(hc == 63),
                                skip_group_check=True)
                            if hh == 15:
                                ins.then_inc(s_dnh, 1)
                    if fc >= 1:
                        stats_mm(sw, 1, fc - 1)
                stats_mm(sw, 1, NFC - 1)
                ab_mm(sw, 1)
        blk.tensor(tensor_body)
    return nc



def kernel(L0_attn_hidden_states, L0_attn_Wq, L0_attn_Wk, L0_attn_Wv, L0_attn_Wo,
           L0_ffn_w_gate, L0_ffn_w_up, L0_ffn_w_down):
    x = np.ascontiguousarray(L0_attn_hidden_states)
    x_fm = np.ascontiguousarray(x.transpose(0, 2, 1))
    gmask, cmask, indsel, ident = _host_consts()
    perm = np.concatenate([np.arange((4 * r + hp) * 128, (4 * r + hp) * 128 + 128)
                           for hp in range(4) for r in range(4)])
    wo_perm = np.ascontiguousarray(L0_attn_Wo[perm])
    wgT = np.ascontiguousarray(L0_ffn_w_gate.T)
    wuT = np.ascontiguousarray(L0_ffn_w_up.T)
    wdT = np.ascontiguousarray(L0_ffn_w_down.T)

    nc = build_fused()
    in_maps = []
    for c in range(8):
        b, hg = c // 4, c % 4
        in_maps.append({
            "x_fm": x_fm[b],
            "wq_s": np.ascontiguousarray(L0_attn_Wq[:, hg * 512:(hg + 1) * 512]),
            "wk_s": np.ascontiguousarray(L0_attn_Wk[:, hg * 512:(hg + 1) * 512]),
            "wv_s": np.ascontiguousarray(L0_attn_Wv[:, hg * 512:(hg + 1) * 512]),
            "gmask": gmask, "cmask": cmask, "indsel": indsel, "ident": ident,
            "x_sl": np.ascontiguousarray(x_fm[b][:, hg * 2048:(hg + 1) * 2048]),
            "wo": wo_perm, "wgT": wgT, "wuT": wuT, "wdT": wdT,
        })
    res = run_bass_kernel_spmd(nc, in_maps, core_ids=list(range(8)))
    out = np.empty((2, S, HID), np.float32)
    for c in range(8):
        b, qtr = c // 4, c % 4
        out[b, qtr * 2048:(qtr + 1) * 2048, :] = res.results[c]["out_fm"].T
    return out
